# version 23
# baseline (speedup 1.0000x reference)
import numpy as np
import ml_dtypes
import concourse.bass as bass
import concourse.mybir as mybir
from concourse.bass import broadcast_tensor_aps
from concourse.bass_utils import run_bass_kernel_spmd

F32 = mybir.dt.float32
I32 = mybir.dt.int32
BF16 = mybir.dt.bfloat16
AF = mybir.ActivationFunctionType
ALU = mybir.AluOpType
AX = mybir.AxisListType

D = 1024
NH = 16
DH = 64
CH = 128
MACRO = 512
N_CORES = 8

f32 = np.float32


class Sched:

    ENG = ("T", "V", "S", "G", "SY")

    def __init__(self):
        self.ops = []
        self.cnt = {"T": 0, "V": 0, "S": 0, "G": 0, "DS": 0, "DG": 0}
        self.writer = {}
        self.readers = {}
        self.needed = {t: set() for t in self.cnt}
        self.sems = None
        self.last_op = {}

    def op(self, stream, fn, reads=(), writes=(), dma=False, fence=False):
        if dma:
            assert stream in ("SY", "G")
            tag = "DS" if stream == "SY" else "DG"
        else:
            assert stream != "SY"
            tag = stream
        need = {}
        for r in reads:
            w = self.writer.get(r)
            if w is not None and w[0] != (stream if not dma else None):
                if w[0] == stream:
                    continue
                need[w[0]] = max(need.get(w[0], 0), w[1])
        for wr in writes:
            for rtag, rcnt in self.readers.get(wr, {}).items():
                if rtag == stream:
                    continue
                need[rtag] = max(need.get(rtag, 0), rcnt)
            pw = self.writer.get(wr)
            if pw is not None and pw[0] != stream:
                need[pw[0]] = max(need.get(pw[0], 0), pw[1])
        for dtag, didx in need.items():
            assert didx <= self.cnt[dtag], (dtag, didx, self.cnt[dtag])
            self.needed[dtag].add(didx)
        lo = self.last_op.get(stream)
        if lo is not None and not dma and stream != "T":
            if lo[2] & (frozenset(reads) | frozenset(writes)):
                if lo[0] not in ("DS", "DG"):
                    self.needed[lo[0]].add(lo[1])
        self.cnt[tag] += 1
        my = self.cnt[tag]
        if fence:
            self.needed[tag].add(my)
        self.ops.append({"stream": stream, "tag": tag, "fn": fn, "dma": dma,
                         "idx": my, "fence": fence, "waits": sorted(need.items())})
        self.last_op[stream] = (tag, my, frozenset(writes))
        for r in reads:
            self.readers.setdefault(r, {})[tag] = my
        for wr in writes:
            self.writer[wr] = (tag, my)
            self.readers[wr] = {}
        return my

    def finalize(self):
        import os
        if os.environ.get("SCHED_ALL_INC"):
            for t in self.needed:
                self.needed[t] = set(range(1, self.cnt[t] + 1))
        for t in ("DS", "DG"):
            self.needed[t] = set(range(1, self.cnt[t] + 1))
        self.rank = {}
        for t, s in self.needed.items():
            order = sorted(s)
            self.rank[t] = {idx: i + 1 for i, idx in enumerate(order)}
        self.total = {t: len(self.needed[t]) for t in self.needed}


def build_program(n_seq=4, chunks_per_seq=16, taps=None):
    assert chunks_per_seq % 4 == 0
    n_chunks = n_seq * chunks_per_seq
    rows = n_seq * chunks_per_seq * CH
    nc = bass.Bass("TRN2", target_bir_lowering=False)

    dx = nc.dram_tensor("x", [rows, D], F32, kind="ExternalInput").ap()
    dwq = nc.dram_tensor("wqT", [D, D], F32, kind="ExternalInput").ap()
    dwk = nc.dram_tensor("wkT", [D, D], F32, kind="ExternalInput").ap()
    dwv = nc.dram_tensor("wvT", [D, D], F32, kind="ExternalInput").ap()
    dwo = nc.dram_tensor("woT", [D, D], F32, kind="ExternalInput").ap()
    drh = nc.dram_tensor("rhs", [D, D], F32, kind="ExternalInput").ap()
    drhT = nc.dram_tensor("rhsT", [D, D], F32, kind="ExternalInput").ap()
    dbq = nc.dram_tensor("bqc", [128, 8], F32, kind="ExternalInput").ap()
    dbk = nc.dram_tensor("bkc", [128, 8], F32, kind="ExternalInput").ap()
    dbv = nc.dram_tensor("bvrep", [128, D], F32, kind="ExternalInput").ap()
    dbo = nc.dram_tensor("borep", [128, D], F32, kind="ExternalInput").ap()
    dmsk = nc.dram_tensor("mask4", [128, 512], F32, kind="ExternalInput").ap()
    didb = nc.dram_tensor("identb", [128, 128], BF16, kind="ExternalInput").ap()
    didf = nc.dram_tensor("identf", [128, 128], F32, kind="ExternalInput").ap()
    damax = nc.dram_tensor("amax_scr", [8, 128], F32, kind="Internal").ap()
    dout = nc.dram_tensor("out", [rows, D], F32, kind="ExternalOutput").ap()

    import contextlib
    es = contextlib.ExitStack()
    sb = lambda name, shape, dt: es.enter_context(nc.sbuf_tensor(name, shape, dt))
    ps = lambda name, shape, dt: es.enter_context(nc.psum_tensor(name, shape, dt))

    wq = sb("wq", [128, 8, D], BF16)
    wk = sb("wk", [128, 8, D], BF16)
    wv = sb("wv", [128, 8, D], BF16)
    wo = sb("wo", [128, 8, D], BF16)
    qrs = sb("qrs", [128, 8, D], BF16)
    rcp_rep = sb("rcp_rep", [128, D], F32)
    rs_rep = sb("rs_rep", [128, D], F32)
    bqc = sb("bqc_s", [128, 8], F32)
    bkc = sb("bkc_s", [128, 8], F32)
    bvr = sb("bvr", [128, D], F32)
    bor = sb("bor", [128, D], F32)
    msk = sb("msk", [128, 4, 128], F32)
    idb = sb("idb", [128, 128], BF16)
    acc8 = sb("acc8", [128, 8], F32)
    acc8m = sb("acc8m", [128, 8], F32)
    xb_ = [sb(f"xb{i}", [128, D], F32) for i in range(2)]
    ob_ = [sb(f"ob{i}", [128, D], F32) for i in range(2)]
    xbo_ = [sb(f"xbo{i}", [128, D], F32) for i in range(2)]
    scr1 = sb("scr1", [128, D], F32)
    ln1_ = [sb(f"ln1_{i}", [128, D], BF16) for i in range(2)]
    l1T = sb("l1T", [128, 8, MACRO], BF16)
    qTm = sb("qTm", [128, 8, MACRO], BF16)
    kTm = sb("kTm", [128, 8, MACRO], BF16)
    vtm_ = [sb(f"vtm{i}", [128, D], BF16) for i in range(4)]
    ktm = sb("ktm", [128, 8, 128], BF16)
    sTm_ = [sb(f"sTm{i}", [128, 8, 128], BF16) for i in range(2)]
    Ssb = sb("Ssb", [128, 8, DH], BF16)
    aT = sb("aT", [128, 8, 128], BF16)
    x2_ = [sb(f"x2_{i}", [128, D], F32) for i in range(3)]
    ln2 = sb("ln2", [128, D], F32)
    yq = sb("yq", [128, D], F32)
    iq = sb("iq", [128, D], I32)
    fq = sb("fq", [128, D], F32)
    qtm_ = [sb(f"qtm{i}", [128, D], BF16) for i in range(3)]
    ls3_ = [sb(f"ls3_{i}", [128, 2], F32) for i in range(3)]
    qTq = sb("qTq", [128, 8, 128], BF16)
    st_ = [sb(f"st{i}", [128, 32], F32) for i in range(2)]


    pA = ps("pA", [128, 8, 128], BF16)
    pB = ps("pB", [128, 512], F32)
    pC = ps("pC", [128, 512], F32)
    pD = ps("pD", [128, D], F32)
    pE = ps("pE", [128, 8, 128], F32)
    pF = ps("pF", [128, 8, DH], F32)

    sd = Sched()
    tap_outs = {}

    def TAP(name, ap, shape, reads):
        if taps is None or name not in taps:
            return
        t = nc.dram_tensor(f"tap_{name}", shape, ap.dtype, kind="ExternalOutput").ap()
        tap_outs[name] = True
        sd.op("SY", lambda e, a=ap, t=t: e.dma_start(t, a), reads=list(reads),
              writes=[f"tapw_{name}"], dma=True)

    C128 = f32(128.0)
    LN127 = f32(np.log(127.0))
    PF128 = 128.0
    EPS = f32(1e-5)
    INV_D = f32(1.0 / D)
    INV127 = f32(1.0 / 127.0)

    sd.op("SY", lambda e: e.dma_start(idb[:], didb), writes=["idb"], dma=True)
    sd.op("SY", lambda e: e.dma_start(msk[:], dmsk.rearrange("p (a f) -> p a f", a=4)),
          writes=["msk"], dma=True)
    sd.op("SY", lambda e: e.dma_start(bqc[:], dbq), writes=["bqc"], dma=True)
    sd.op("SY", lambda e: e.dma_start(bkc[:], dbk), writes=["bkc"], dma=True)
    sd.op("SY", lambda e: e.dma_start(bvr[:], dbv), writes=["bvr"], dma=True)
    sd.op("SY", lambda e: e.dma_start(bor[:], dbo), writes=["bor"], dma=True)
    for nm, dst, src in (("wq", wq, dwq), ("wk", wk, dwk), ("wv", wv, dwv), ("wo", wo, dwo)):
        sd.op("G", lambda e, d=dst, s=src: e.dma_start(
            d[:], s.rearrange("(k p) e -> p k e", p=128)), writes=[nm], dma=True)


    slabbuf = [scr1, x2_[0]]
    slabres = ["scr1", "x2_0"]
    for k in range(8):
        b = k % 2
        sd.op("SY", lambda e, k=k, d=slabbuf[b]: e.dma_start(
            d[:], drhT[128 * k:128 * (k + 1), :]), writes=[slabres[b]], dma=True)
        sd.op("V", lambda e, k=k, b=b: e.reduce_max(
            acc8[:, k:k + 1], slabbuf[b][:], AX.X, apply_absolute_value=True),
            reads=[slabres[b]], writes=[f"acc8_{k}"])
    sd.op("V", lambda e: e.tensor_scalar_max(acc8m[:], acc8[:], 1e-12),
          reads=[f"acc8_{j}" for j in range(8)] + ["acc8"], writes=["acc8m"])
    TAP("acc8", acc8[:], [128, 8], [f"acc8_{j}" for j in range(8)])
    TAP("acc8m", acc8m[:], [128, 8], ["acc8m"])
    for j in range(8):
        sd.op("SY", lambda e, j=j: e.dma_start(
            damax[j:j + 1, :], acc8m[:, j:j + 1],
            allow_slow_non_contiguous=True),
            reads=["acc8m"], writes=[f"damax{j}"], dma=True)
    row3 = damax.rearrange("j q -> () j q")
    def _bcast_load(e):
        d_ap = rcp_rep[:, :].rearrange("p (j q) -> p j q", j=8)
        d_ap, s_ap = broadcast_tensor_aps(d_ap, row3)
        return e.dma_start(d_ap, s_ap)
    sd.op("SY", _bcast_load, reads=[f"damax{j}" for j in range(8)],
          writes=["rcp_rep"], dma=True)
    TAP("amax_rep", rcp_rep[:], [128, D], ["rcp_rep"])
    sd.op("V", lambda e: e.tensor_scalar_mul(rs_rep[:], rcp_rep[:], float(INV127)),
          reads=["rcp_rep"], writes=["rs_rep"])
    sd.op("S", lambda e: e.activation(scr1[:], rcp_rep[:], AF.Ln,
          bias=f32(0.0), scale=f32(1.0)), reads=["rcp_rep", "scr1"], writes=["scr1"])
    sd.op("S", lambda e: e.activation(rcp_rep[:], scr1[:], AF.Exp,
          bias=LN127, scale=f32(-1.0)), reads=["scr1"], writes=["rcp_rep"])
    for k in range(8):
        b = k % 2
        sd.op("SY", lambda e, k=k, d=slabbuf[b]: e.dma_start(
            d[:], drh[128 * k:128 * (k + 1), :]), writes=[slabres[b]], dma=True)
        pdq = ln2
        sd.op("V", lambda e, b=b: e.tensor_mul(yq[:], slabbuf[b][:], rcp_rep[:]),
              reads=[slabres[b], "rcp_rep"], writes=["yq"])
        sd.op("V", lambda e, d=pdq: e.tensor_scalar_add(d[:], yq[:], PF128),
              reads=["yq"], writes=["ln2x"])
        sd.op("V", lambda e, d=pdq: e.tensor_copy(iq[:], d[:]), reads=["ln2x"], writes=["iq"])
        sd.op("G", lambda e: e.tensor_copy(fq[:], iq[:]), reads=["iq"], writes=["fq"])
        sd.op("G", lambda e, d=pdq: e.tensor_sub(yq[:], d[:], fq[:]),
              reads=["ln2x", "fq"], writes=["yq"])
        sd.op("V", lambda e: e.tensor_scalar(iq[:].bitcast(F32), yq[:], 0.5, -128.0, ALU.is_ge, ALU.add),
              reads=["yq"], writes=["iq"])
        sd.op("G", lambda e, k=k: e.tensor_add(qrs[:, k, :], fq[:], iq[:].bitcast(F32)),
              reads=["fq", "iq"], writes=[f"qrs_{k}"])
    TAP("qrs", qrs[:], [128, 8, D], [f"qrs_{k}" for k in range(8)])
    TAP("rcp_rep", rcp_rep[:], [128, D], ["rcp_rep"])
    TAP("rs_rep", rs_rep[:], [128, D], ["rs_rep"])

    def emit_mid_stats(c, p):
        q3 = c % 3
        stt = st_[p]
        stn = f"st{p}"
        x2n, l2n = f"x2_{q3}", "ln2r"
        x2t = x2_[q3]
        sd.op("V", lambda e, s=stt, x2t=x2t: e.scalar_tensor_tensor(
            scr1[:], x2t[:], 1.0, x2t[:], ALU.bypass, ALU.mult,
            accum_out=s[:, 14:15]), reads=[x2n, stn], writes=["scr1", stn])
        sd.op("S", lambda e, s=stt: e.activation(
            s[:, 15:16], s[:, 8:9], AF.Square, bias=f32(0.0), scale=f32(INV_D)),
            reads=[stn], writes=[stn])
        sd.op("V", lambda e, s=stt: e.scalar_tensor_tensor(
            s[:, 10:11], s[:, 14:15], float(INV_D), s[:, 15:16], ALU.mult, ALU.subtract),
            reads=[stn], writes=[stn])
        sd.op("S", lambda e, s=stt: e.activation(
            s[:, 11:12], s[:, 10:11], AF.Ln, bias=EPS, scale=f32(1.0)),
            reads=[stn], writes=[stn])
        sd.op("S", lambda e, s=stt: e.activation(
            s[:, 12:13], s[:, 11:12], AF.Exp, bias=f32(0.0), scale=f32(-0.5)),
            reads=[stn], writes=[stn])
        sd.op("V", lambda e, s=stt: e.scalar_tensor_tensor(
            s[:, 13:14], s[:, 8:9], float(-INV_D), s[:, 12:13], ALU.mult, ALU.mult),
            reads=[stn], writes=[stn])
        sd.op("S", lambda e, s=stt, x2t=x2t: e.activation(
            ln2[:], x2t[:], AF.Identity, bias=s[:, 13:14], scale=s[:, 12:13]),
            reads=[x2n, stn], writes=[l2n])
        TAP(f"ln2_c{c}", ln2[:], [128, D], [l2n])

    def emit_mid_statsB(c, p):
        q3 = c % 3
        stt = st_[p]
        stn = f"st{p}"
        l2n = "ln2r"
        sd.op("V", lambda e, s=stt: e.reduce_max(
            s[:, 16:17], ln2[:], AX.X, apply_absolute_value=True),
            reads=[l2n], writes=[stn])
        sd.op("V", lambda e, s=stt: e.tensor_scalar_max(
            s[:, 17:18], s[:, 16:17], 1e-12), reads=[stn], writes=[stn])
        sd.op("S", lambda e, s=stt: e.activation(
            s[:, 20:21], s[:, 17:18], AF.Ln, bias=f32(0.0), scale=f32(1.0)),
            reads=[stn], writes=[stn])
        sd.op("S", lambda e, s=stt: e.activation(
            s[:, 18:19], s[:, 20:21], AF.Exp, bias=LN127, scale=f32(-1.0)),
            reads=[stn], writes=[stn])
        sd.op("V", lambda e, s=stt, q3=q3: e.tensor_scalar_mul(
            ls3_[q3][:, 0:1], s[:, 17:18], float(INV127)), reads=[stn],
            writes=[f"ls3_{q3}"])
        sd.op("S", lambda e, s=stt: e.activation(
            yq[:], ln2[:], AF.Identity, bias=C128, scale=s[:, 18:19]),
            reads=[l2n, stn], writes=["yq"])
        TAP(f"st2_c{c}", stt[:], [128, 32], [stn])

    def emit_mid_quant(c, p):
        q3 = c % 3
        sd.op("V", lambda e: e.tensor_copy(iq[:], yq[:]), reads=["yq"], writes=["iq"])
        sd.op("G", lambda e: e.tensor_copy(fq[:], iq[:]), reads=["iq"], writes=["fq"])
        sd.op("V", lambda e: e.tensor_sub(scr1[:], yq[:], fq[:]),
              reads=["yq", "fq"], writes=["scr1"])
        sd.op("V", lambda e: e.tensor_scalar(
            iq[:].bitcast(F32), scr1[:], 0.5, -128.0, ALU.is_ge, ALU.add),
            reads=["scr1"], writes=["iq"])
        sd.op("G", lambda e, q3=q3: e.tensor_add(qtm_[q3][:], fq[:], iq[:].bitcast(F32)),
              reads=["fq", "iq"], writes=[f"qtm{q3}"])
        TAP(f"qtm_c{c}", qtm_[q3][:], [128, D], [f"qtm{q3}"])

    def emit_tail(c, cc, p, r0):
        q3 = c % 3
        x2n = f"x2_{q3}"
        for j in range(8):
            sd.op("T", lambda e, j=j, q3=q3: e.transpose(
                pA[:, j, :], qtm_[q3][:, 128 * j:128 * (j + 1)], idb[:]),
                reads=[f"qtm{q3}", "idb"], writes=["pA"])
        sd.op("S", lambda e: e.copy(qTq[:, :, :], pA[:, :, :]),
              reads=["pA"], writes=["qTq"])
        for k in range(8):
            sd.op("T", lambda e, k=k: e.matmul(
                pB[:, 0:512], qTq[:, k, :], qrs[:, k, 0:512],
                start=(k == 0), stop=(k == 7)),
                reads=["qTq", f"qrs_{k}"], writes=["pB"])
            sd.op("T", lambda e, k=k: e.matmul(
                pC[:, 0:512], qTq[:, k, :], qrs[:, k, 512:1024],
                start=(k == 0), stop=(k == 7)),
                reads=["qTq", f"qrs_{k}"], writes=["pC"])
        sd.op("V", lambda e: e.tensor_mul(scr1[:, 0:512], pB[:, :], rs_rep[:, 0:512]),
              reads=["pB", "rs_rep"], writes=["scr1"])
        sd.op("V", lambda e: e.tensor_mul(scr1[:, 512:1024], pC[:, :], rs_rep[:, 512:1024]),
              reads=["pC", "rs_rep"], writes=["scr1"])
        sd.op("V", lambda e, p=p, q3=q3: e.scalar_tensor_tensor(
            ob_[p][:], scr1[:], ls3_[q3][:, 0:1], x2_[q3][:], ALU.mult, ALU.add),
            reads=["scr1", f"ls3_{q3}", x2n], writes=[f"ob{p}"])
        sd.op("SY", lambda e, r0=r0, p=p: e.dma_start(
            dout[r0:r0 + CH, :], ob_[p][:]), reads=[f"ob{p}"], writes=[f"outd{c}"],
            dma=True)

    pending_tail = [None, None]
    pending_mid = [None]

    HB = [64 * (h % 2) for h in range(NH)]
    HS = [h // 2 for h in range(NH)]

    def emit_xdma(c):
        p = c % 2
        r0 = c * CH
        sd.op("SY", lambda e, r0=r0, p=p: e.dma_start(
            xb_[p][:], dx[r0:r0 + CH, :]), writes=[f"xb{p}"], dma=True)

    def emit_ln1_chain(c):
        p = c % 2
        stt = st_[p]
        stn = f"st{p}"
        xr, lr = f"xb{p}", f"ln1_{p}"
        sd.op("V", lambda e, p=p, s=stt: e.reduce_sum(s[:, 0:1], xb_[p][:], AX.X),
              reads=[xr], writes=[stn])
        sd.op("V", lambda e, p=p, s=stt: e.scalar_tensor_tensor(
            scr1[:], xb_[p][:], 1.0, xb_[p][:], ALU.bypass, ALU.mult,
            accum_out=s[:, 6:7]), reads=[xr, stn], writes=["scr1", stn])
        sd.op("G", lambda e, p=p: e.tensor_add(xbo_[p][:], xb_[p][:], bor[:]),
              reads=[xr, "bor"], writes=[f"xbo{p}"])
        sd.op("S", lambda e, s=stt: e.activation(
            s[:, 7:8], s[:, 0:1], AF.Square, bias=f32(0.0), scale=f32(INV_D)),
            reads=[stn], writes=[stn])
        sd.op("V", lambda e, s=stt: e.scalar_tensor_tensor(
            s[:, 2:3], s[:, 6:7], float(INV_D), s[:, 7:8], ALU.mult, ALU.subtract),
            reads=[stn], writes=[stn])
        sd.op("S", lambda e, s=stt: e.activation(
            s[:, 3:4], s[:, 2:3], AF.Ln, bias=EPS, scale=f32(1.0)),
            reads=[stn], writes=[stn])
        sd.op("S", lambda e, s=stt: e.activation(
            s[:, 4:5], s[:, 3:4], AF.Exp, bias=f32(0.0), scale=f32(-0.5)),
            reads=[stn], writes=[stn])
        sd.op("V", lambda e, s=stt: e.scalar_tensor_tensor(
            s[:, 5:6], s[:, 0:1], float(-INV_D), s[:, 4:5], ALU.mult, ALU.mult),
            reads=[stn], writes=[stn])
        sd.op("S", lambda e, p=p, s=stt: e.activation(
            ln1_[p][:], xb_[p][:], AF.Identity, bias=s[:, 5:6], scale=s[:, 4:5]),
            reads=[xr, stn], writes=[lr])
        TAP(f"ln1_c{c}", ln1_[p][:], [128, D], [lr])
        TAP(f"st_c{c}", stt[:], [128, 32], [stn])

    def emit_ln1_T(c):
        p = c % 2
        cc = c % 4
        lr = f"ln1_{p}"
        for j in range(8):
            sd.op("T", lambda e, p=p, j=j: e.transpose(
                pA[:, j, :], ln1_[p][:, 128 * j:128 * (j + 1)], idb[:]),
                reads=[lr, "idb"], writes=["pA"])
        sd.op("S", lambda e, cc=cc: e.copy(
            l1T[:, :, 128 * cc:128 * (cc + 1)], pA[:, :, :]),
            reads=["pA"], writes=[f"l1T_{cc}"])
        TAP(f"l1T_c{c}", l1T[:, :, 128 * cc:128 * (cc + 1)], [128, 8, 128], [f"l1T_{cc}"])

    l1T_all = [f"l1T_{cc}" for cc in range(4)]
    n_macros = n_chunks // 4
    for m in range(n_macros):
        if m == 0:
            for cc in range(4):
                emit_xdma(cc)
                emit_ln1_chain(cc)
                emit_ln1_T(cc)
        for j in range(8):
            for pi, (w, wn, dst, dn, bc, bcn) in enumerate(
                    ((wq, "wq", qTm, "qTm", bqc, "bqc"),
                     (wk, "wk", kTm, "kTm", bkc, "bkc"))):
                bank, bn = (pB, "pB") if (2 * j + pi) % 2 == 0 else (pC, "pC")
                for k in range(8):
                    sd.op("T", lambda e, j=j, k=k, w=w, b=bank: e.matmul(
                        b[:, 0:512], w[:, k, 128 * j:128 * (j + 1)],
                        l1T[:, k, 0:512], start=(k == 0), stop=(k == 7)),
                        reads=[wn] + l1T_all, writes=[bn])
                sd.op("S", lambda e, j=j, d=dst, b=bank, bc=bc: e.activation(
                    d[:, j, :], b[:, 0:512], AF.Identity, bias=bc[:, j:j + 1],
                    scale=f32(1.0)), reads=[bn, bcn], writes=[f"{dn}_{j}"])
        qTm_all = [f"qTm_{j}" for j in range(8)]
        kTm_all = [f"kTm_{j}" for j in range(8)]
        TAP(f"qTm_m{m}", qTm[:], [128, 8, MACRO], qTm_all)
        TAP(f"kTm_m{m}", kTm[:], [128, 8, MACRO], kTm_all)

        for cc in range(4):
            c = 4 * m + cc
            p = c % 2
            cs = c % chunks_per_seq
            r0 = c * CH
            stt = st_[p]
            stn = f"st{p}"
            vtn = f"vtm{cc}"
            if m + 1 < n_macros:
                emit_xdma(c + 4)
            for k in range(8):
                for h2 in range(2):
                    sd.op("T", lambda e, cc=cc, k=k, h2=h2: e.matmul(
                        pD[:, 512 * h2:512 * (h2 + 1)],
                        l1T[:, k, 128 * cc:128 * (cc + 1)],
                        wv[:, k, 512 * h2:512 * (h2 + 1)],
                        start=(k == 0), stop=(k == 7)),
                        reads=["wv", f"l1T_{cc}"], writes=["pD"])
            sd.op("V", lambda e, cc=cc: e.scalar_tensor_tensor(
                vtm_[cc][:], pD[:, :], 1.0, bvr[:], ALU.bypass, ALU.add),
                reads=["pD", "bvr"], writes=[vtn])
            TAP(f"vtm_c{c}", vtm_[cc][:], [128, D], [vtn])
            for j in range(8):
                sd.op("T", lambda e, cc=cc, j=j: e.transpose(
                    pA[:, j, :], kTm[:, j, 128 * cc:128 * (cc + 1)], idb[:]),
                    reads=[f"kTm_{j}", "idb"], writes=["pA"])
            sd.op("S", lambda e: e.copy(ktm[:, :, :], pA[:, :, :]),
                  reads=["pA"], writes=["ktm"])
            TAP(f"ktm_c{c}", ktm[:], [128, 8, 128], ["ktm"])
            if cs > 0:
                sd.op("S", lambda e: e.copy(Ssb[:, :, :], pF[:, :, :]),
                      reads=["pF"], writes=["Ssb"])
            for lh in range(8):
                h = lh
                bank, bn = (pB, "pB") if h % 2 == 0 else (pC, "pC")
                a = lh // 2
                sd.op("T", lambda e, cc=cc, h=h, b=bank, a=a: e.matmul(
                    b[:, 128 * a:128 * (a + 1)],
                    kTm[HB[h]:HB[h] + DH, HS[h], 128 * cc:128 * (cc + 1)],
                    qTm[HB[h]:HB[h] + DH, HS[h], 128 * cc:128 * (cc + 1)],
                    start=True, stop=True,
                    tile_position=(HB[h], 0), skip_group_check=True),
                    reads=[f"kTm_{HS[h]}", f"qTm_{HS[h]}"], writes=[bn])
            for lh in range(8):
                h = 8 + lh
                a = (lh // 2) + (0 if lh % 2 == 0 else 4)
                sd.op("T", lambda e, cc=cc, h=h, a=a: e.matmul(
                    pD[:, 128 * a:128 * (a + 1)],
                    kTm[HB[h]:HB[h] + DH, HS[h], 128 * cc:128 * (cc + 1)],
                    qTm[HB[h]:HB[h] + DH, HS[h], 128 * cc:128 * (cc + 1)],
                    start=True, stop=True,
                    tile_position=(HB[h], 0), skip_group_check=True),
                    reads=[f"kTm_{HS[h]}", f"qTm_{HS[h]}"], writes=["pD"])
            sd.op("V", lambda e: e.tensor_tensor(
                sTm_[0][:, 0:4, :], pB[:, :].rearrange("p (a f) -> p a f", a=4),
                msk[:, :, :], ALU.mult),
                reads=["pB", "msk"], writes=["sTm0"])
            sd.op("V", lambda e: e.tensor_tensor(
                sTm_[0][:, 4:8, :], pC[:, :].rearrange("p (a f) -> p a f", a=4),
                msk[:, :, :], ALU.mult),
                reads=["pC", "msk"], writes=["sTm0"])
            sd.op("V", lambda e: e.tensor_tensor(
                sTm_[1][:, 0:4, :], pD[:, 0:512].rearrange("p (a f) -> p a f", a=4),
                msk[:, :, :], ALU.mult),
                reads=["pD", "msk"], writes=["sTm1"])
            sd.op("V", lambda e: e.tensor_tensor(
                sTm_[1][:, 4:8, :], pD[:, 512:1024].rearrange("p (a f) -> p a f", a=4),
                msk[:, :, :], ALU.mult),
                reads=["pD", "msk"], writes=["sTm1"])

            for h in range(NH):
                w = h // 8
                lh = h - 8 * w
                slab = (lh // 2) + (0 if lh % 2 == 0 else 4)
                first = h in (0, 1, 8, 9)
                sd.op("T", lambda e, cc=cc, h=h, first=first, w=w, sl=slab: e.matmul(
                    pE[HB[h]:HB[h] + DH, HS[h], :],
                    vtm_[cc][:, DH * h:DH * (h + 1)],
                    sTm_[w][:, sl, :],
                    start=first, stop=(cs == 0),
                    tile_position=(0, HB[h]), skip_group_check=True),
                    reads=[vtn, f"sTm{w}"], writes=["pE"])
                if cs > 0:
                    sd.op("T", lambda e, cc=cc, h=h: e.matmul(
                        pE[HB[h]:HB[h] + DH, HS[h], :],
                        Ssb[HB[h]:HB[h] + DH, HS[h], :],
                        qTm[HB[h]:HB[h] + DH, HS[h], 128 * cc:128 * (cc + 1)],
                        start=False, stop=True,
                        tile_position=(HB[h], HB[h]), skip_group_check=True),
                        reads=["Ssb", f"qTm_{HS[h]}"], writes=["pE"])
            for h in range(NH):
                sd.op("T", lambda e, cc=cc, h=h, cs=cs: e.matmul(
                    pF[HB[h]:HB[h] + DH, HS[h], :],
                    ktm[:, HS[h], HB[h]:HB[h] + DH],
                    vtm_[cc][:, DH * h:DH * (h + 1)],
                    start=(cs == 0 and h in (0, 1)),
                    stop=(cs == chunks_per_seq - 1 and h in (NH - 2, NH - 1)),
                    tile_position=(0, HB[h]), skip_group_check=True),
                    reads=["ktm", vtn], writes=["pF"])
            sd.op("S", lambda e: e.copy(aT[:, :, :], pE[:, :, :]),
                  reads=["pE"], writes=["aT"])
            if pending_mid[0] is not None:
                emit_mid_stats(*pending_mid[0])
                emit_mid_statsB(*pending_mid[0])
            TAP(f"sTm0_c{c}", sTm_[0][:], [128, 8, 128], ["sTm0"])
            TAP(f"sTm1_c{c}", sTm_[1][:], [128, 8, 128], ["sTm1"])
            TAP(f"aT_c{c}", aT[:], [128, 8, 128], ["aT"])
            if cs > 0:
                TAP(f"Ssb_c{c}", Ssb[:], [128, 8, DH], ["Ssb"])
            if m + 1 < n_macros:
                emit_ln1_chain(c + 4)
            if pending_mid[0] is not None:
                emit_mid_quant(*pending_mid[0])
                pending_mid[0] = None
            for k in range(8):
                for h2 in range(2):
                    sd.op("T", lambda e, k=k, h2=h2: e.matmul(
                        pD[:, 512 * h2:512 * (h2 + 1)], aT[:, k, :],
                        wo[:, k, 512 * h2:512 * (h2 + 1)],
                        start=(k == 0), stop=(k == 7)),
                        reads=["aT", "wo"], writes=["pD"])
            q3 = c % 3
            sd.op("V", lambda e, p=p, q3=q3, s=stt: e.scalar_tensor_tensor(
                x2_[q3][:], pD[:, :], 1.0, xbo_[p][:], ALU.bypass, ALU.add,
                accum_out=s[:, 8:9]), reads=["pD", f"xbo{p}", stn],
                writes=[f"x2_{q3}", stn])
            TAP(f"x2_c{c}", x2_[q3][:], [128, D], [f"x2_{q3}"])
            if pending_tail[0] is not None:
                emit_tail(*pending_tail[0])
            if m + 1 < n_macros:
                emit_ln1_T(c + 4)
            pending_mid[0] = (c, p)
            pending_tail[0] = pending_tail[1]
            pending_tail[1] = (c, cc, p, r0)

    if pending_mid[0] is not None:
        emit_mid_stats(*pending_mid[0])
        emit_mid_statsB(*pending_mid[0])
        emit_mid_quant(*pending_mid[0])
        pending_mid[0] = None
    if pending_tail[0] is not None:
        emit_tail(*pending_tail[0])
    if pending_tail[1] is not None:
        emit_tail(*pending_tail[1])

    sd.finalize()
    with (
        nc.semaphore("s_t") as s_t,
        nc.semaphore("s_v") as s_v,
        nc.semaphore("s_s") as s_s,
        nc.semaphore("s_g") as s_g,
        nc.semaphore("s_ds") as s_ds,
        nc.semaphore("s_dg") as s_dg,
        nc.Block() as block,
    ):
        sd.sems = {"T": s_t, "V": s_v, "S": s_s, "G": s_g, "DS": s_ds, "DG": s_dg}
        total_ds = 16 * sd.total["DS"]

        def run_stream(eng_obj, stream):
            hw = {}
            for o in sd.ops:
                if o["stream"] != stream:
                    continue
                for dtag, didx in o["waits"]:
                    rk = sd.rank[dtag][didx]
                    if hw.get(dtag, 0) >= rk:
                        continue
                    hw[dtag] = rk
                    val = 16 * rk if dtag in ("DS", "DG") else rk
                    eng_obj.wait_ge(sd.sems[dtag], val)
                tag = o["tag"]
                if o["idx"] in sd.needed[tag]:
                    rk = sd.rank[tag][o["idx"]]
                    if o["dma"]:
                        o["fn"](eng_obj).then_inc(sd.sems[tag], 16)
                    elif stream == "T":
                        o["fn"](eng_obj).then_inc(sd.sems[tag], 1)
                    else:
                        o["fn"](eng_obj).then_inc(sd.sems[tag], 1)
                        eng_obj.wait_ge(sd.sems[tag], rk)
                else:
                    o["fn"](eng_obj)

        @block.tensor
        def _(t):
            run_stream(t, "T")

        @block.vector
        def _(v):
            run_stream(v, "V")

        @block.scalar
        def _(s):
            run_stream(s, "S")

        @block.gpsimd
        def _(g):
            run_stream(g, "G")

        @block.sync
        def _(sy):
            run_stream(sy, "SY")
            sy.wait_ge(s_ds, total_ds)

    es.close()
    return nc


def _host_inputs(inputs, x_shard):
    wq = np.ascontiguousarray(inputs["L0_attn_wq_w"].T)
    wk = np.ascontiguousarray(inputs["L0_attn_wk_w"].T)
    wv = np.ascontiguousarray(inputs["L0_attn_wv_w"].T)
    wo = np.ascontiguousarray(inputs["L0_attn_wo_w"].T)
    rhs = np.ascontiguousarray(inputs["L0_ffn_rhs"])
    bq = np.ascontiguousarray(inputs["L0_attn_wq_b"].reshape(8, 128).T)
    bk = np.ascontiguousarray(inputs["L0_attn_wk_b"].reshape(8, 128).T)
    bvrep = np.ascontiguousarray(
        np.broadcast_to(inputs["L0_attn_wv_b"][None, :], (128, D)))
    borep = np.ascontiguousarray(
        np.broadcast_to(inputs["L0_attn_wo_b"][None, :], (128, D)))
    mask = np.triu(np.ones((128, 128), dtype=np.float32))
    mask4 = np.ascontiguousarray(np.tile(mask, (1, 4)))
    identb = np.eye(128).astype(ml_dtypes.bfloat16)
    return {
        "x": x_shard, "wqT": wq, "wkT": wk, "wvT": wv, "woT": wo, "rhs": rhs,
        "bqc": bq, "bkc": bk, "bvrep": bvrep, "borep": borep,
        "mask4": mask4, "identb": identb, "identf": np.eye(128, dtype=np.float32),
        "rhsT": np.ascontiguousarray(rhs.T),
    }


def kernel(**inputs):
    x = np.asarray(inputs["L0_attn_x"], dtype=np.float32)
    B, L, d = x.shape
    assert (B, L, d) == (32, 2048, 1024)
    xf = np.ascontiguousarray(x.reshape(B * L, d))
    seq_per_core = B // N_CORES
    rows_per_core = seq_per_core * L
    nc = build_program(n_seq=seq_per_core, chunks_per_seq=L // CH)
    in_maps = []
    for ci in range(N_CORES):
        shard = np.ascontiguousarray(
            xf[ci * rows_per_core:(ci + 1) * rows_per_core])
        in_maps.append(_host_inputs(inputs, shard))
    res = run_bass_kernel_spmd(nc, in_maps, core_ids=list(range(N_CORES)))
    out = np.concatenate([np.asarray(r["out"]) for r in res.results], axis=0)
    return out.reshape(B, L, d).astype(np.float32)


if __name__ == "__main__":
    rng = np.random.RandomState(0)
    ins = {
        "L0_attn_x": rng.randn(32, 2048, 1024).astype(np.float32),
        "L0_attn_wq_w": (rng.randn(1024, 1024) / 32).astype(np.float32),
        "L0_attn_wq_b": (rng.randn(1024) * 0.01).astype(np.float32),
        "L0_attn_wk_w": (rng.randn(1024, 1024) / 32).astype(np.float32),
        "L0_attn_wk_b": (rng.randn(1024) * 0.01).astype(np.float32),
        "L0_attn_wv_w": (rng.randn(1024, 1024) / 32).astype(np.float32),
        "L0_attn_wv_b": (rng.randn(1024) * 0.01).astype(np.float32),
        "L0_attn_wo_w": (rng.randn(1024, 1024) / 32).astype(np.float32),
        "L0_attn_wo_b": (rng.randn(1024) * 0.01).astype(np.float32),
        "L0_ffn_rhs": rng.randn(1024, 1024).astype(np.float32),
    }
    out = kernel(**ins)
    print(out.shape, out.dtype)


# version 24 (final)
# speedup vs baseline: 1.0894x; 1.0894x over previous
import numpy as np
import ml_dtypes
import concourse.bass as bass
import concourse.mybir as mybir
from concourse.bass import broadcast_tensor_aps
from concourse.bass_utils import run_bass_kernel_spmd

F32 = mybir.dt.float32
I32 = mybir.dt.int32
BF16 = mybir.dt.bfloat16
AF = mybir.ActivationFunctionType
ALU = mybir.AluOpType
AX = mybir.AxisListType

D = 1024
NH = 16
DH = 64
CH = 128
MACRO = 512
N_CORES = 8

f32 = np.float32


class Sched:

    ENG = ("T", "V", "S", "G", "SY")

    def __init__(self):
        self.ops = []
        self.cnt = {"T": 0, "V": 0, "S": 0, "G": 0, "DS": 0, "DG": 0}
        self.writer = {}
        self.readers = {}
        self.needed = {t: set() for t in self.cnt}
        self.sems = None
        self.last_op = {}

    def op(self, stream, fn, reads=(), writes=(), dma=False, fence=False):
        if dma:
            assert stream in ("SY", "G")
            tag = "DS" if stream == "SY" else "DG"
        else:
            assert stream != "SY"
            tag = stream
        need = {}
        for r in reads:
            w = self.writer.get(r)
            if w is not None and w[0] != (stream if not dma else None):
                if w[0] == stream:
                    continue
                need[w[0]] = max(need.get(w[0], 0), w[1])
        for wr in writes:
            for rtag, rcnt in self.readers.get(wr, {}).items():
                if rtag == stream:
                    continue
                need[rtag] = max(need.get(rtag, 0), rcnt)
            pw = self.writer.get(wr)
            if pw is not None and pw[0] != stream:
                need[pw[0]] = max(need.get(pw[0], 0), pw[1])
        for dtag, didx in need.items():
            assert didx <= self.cnt[dtag], (dtag, didx, self.cnt[dtag])
            self.needed[dtag].add(didx)
        lo = self.last_op.get(stream)
        if lo is not None and not dma and stream != "T":
            if lo[2] & (frozenset(reads) | frozenset(writes)):
                if lo[0] not in ("DS", "DG"):
                    self.needed[lo[0]].add(lo[1])
        self.cnt[tag] += 1
        my = self.cnt[tag]
        if fence:
            self.needed[tag].add(my)
        self.ops.append({"stream": stream, "tag": tag, "fn": fn, "dma": dma,
                         "idx": my, "fence": fence, "waits": sorted(need.items())})
        self.last_op[stream] = (tag, my, frozenset(writes))
        for r in reads:
            self.readers.setdefault(r, {})[tag] = my
        for wr in writes:
            self.writer[wr] = (tag, my)
            self.readers[wr] = {}
        return my

    def finalize(self):
        import os
        if os.environ.get("SCHED_ALL_INC"):
            for t in self.needed:
                self.needed[t] = set(range(1, self.cnt[t] + 1))
        for t in ("DS", "DG"):
            self.needed[t] = set(range(1, self.cnt[t] + 1))
        self.rank = {}
        for t, s in self.needed.items():
            order = sorted(s)
            self.rank[t] = {idx: i + 1 for i, idx in enumerate(order)}
        self.total = {t: len(self.needed[t]) for t in self.needed}


def build_program(n_seq=4, chunks_per_seq=16, taps=None):
    assert chunks_per_seq % 4 == 0
    n_chunks = n_seq * chunks_per_seq
    rows = n_seq * chunks_per_seq * CH
    nc = bass.Bass("TRN2", target_bir_lowering=False)

    dx = nc.dram_tensor("x", [rows, D], F32, kind="ExternalInput").ap()
    dwq = nc.dram_tensor("wqT", [D, D], F32, kind="ExternalInput").ap()
    dwk = nc.dram_tensor("wkT", [D, D], F32, kind="ExternalInput").ap()
    dwv = nc.dram_tensor("wvT", [D, D], F32, kind="ExternalInput").ap()
    dwo = nc.dram_tensor("woT", [D, D], F32, kind="ExternalInput").ap()
    drh = nc.dram_tensor("rhs", [D, D], F32, kind="ExternalInput").ap()
    drhT = nc.dram_tensor("rhsT", [D, D], F32, kind="ExternalInput").ap()
    dbq = nc.dram_tensor("bqc", [128, 8], F32, kind="ExternalInput").ap()
    dbk = nc.dram_tensor("bkc", [128, 8], F32, kind="ExternalInput").ap()
    dbv = nc.dram_tensor("bvrep", [128, D], F32, kind="ExternalInput").ap()
    dbo = nc.dram_tensor("borep", [128, D], F32, kind="ExternalInput").ap()
    dmsk = nc.dram_tensor("mask4", [128, 512], F32, kind="ExternalInput").ap()
    didb = nc.dram_tensor("identb", [128, 128], BF16, kind="ExternalInput").ap()
    didf = nc.dram_tensor("identf", [128, 128], F32, kind="ExternalInput").ap()
    damax = nc.dram_tensor("amax_scr", [8, 128], F32, kind="Internal").ap()
    dout = nc.dram_tensor("out", [rows, D], F32, kind="ExternalOutput").ap()

    import contextlib
    es = contextlib.ExitStack()
    sb = lambda name, shape, dt: es.enter_context(nc.sbuf_tensor(name, shape, dt))
    ps = lambda name, shape, dt: es.enter_context(nc.psum_tensor(name, shape, dt))

    wq = sb("wq", [128, 8, D], BF16)
    wk = sb("wk", [128, 8, D], BF16)
    wv = sb("wv", [128, 8, D], BF16)
    wo = sb("wo", [128, 8, D], BF16)
    qrs = sb("qrs", [128, 8, D], BF16)
    rcp_rep = sb("rcp_rep", [128, D], F32)
    rs_rep = sb("rs_rep", [128, D], F32)
    bqc = sb("bqc_s", [128, 8], F32)
    bkc = sb("bkc_s", [128, 8], F32)
    bvr = sb("bvr", [128, D], F32)
    bor = sb("bor", [128, D], F32)
    msk = sb("msk", [128, 4, 128], F32)
    idb = sb("idb", [128, 128], BF16)
    acc8 = sb("acc8", [128, 8], F32)
    acc8m = sb("acc8m", [128, 8], F32)
    xb_ = [sb(f"xb{i}", [128, D], F32) for i in range(2)]
    ob_ = [sb(f"ob{i}", [128, D], F32) for i in range(2)]
    xbo_ = [sb(f"xbo{i}", [128, D], F32) for i in range(2)]
    scr1 = sb("scr1", [128, D], F32)
    ln1_ = [sb(f"ln1_{i}", [128, D], BF16) for i in range(2)]
    l1T = sb("l1T", [128, 8, MACRO], BF16)
    qTm = sb("qTm", [128, 8, MACRO], BF16)
    kTm = sb("kTm", [128, 8, MACRO], BF16)
    vtm_ = [sb(f"vtm{i}", [128, D], BF16) for i in range(4)]
    ktm = sb("ktm", [128, 8, 128], BF16)
    sTm_ = [sb(f"sTm{i}", [128, 8, 128], BF16) for i in range(2)]
    Ssb = sb("Ssb", [128, 8, DH], BF16)
    aT = sb("aT", [128, 8, 128], BF16)
    x2_ = [sb(f"x2_{i}", [128, D], F32) for i in range(3)]
    ln2 = sb("ln2", [128, D], F32)
    yq = sb("yq", [128, D], F32)
    iq = sb("iq", [128, D], I32)
    fq = sb("fq", [128, D], F32)
    qtm_ = [sb(f"qtm{i}", [128, D], BF16) for i in range(3)]
    ls3_ = [sb(f"ls3_{i}", [128, 2], F32) for i in range(3)]
    qTq = sb("qTq", [128, 8, 128], BF16)
    st_ = [sb(f"st{i}", [128, 32], F32) for i in range(2)]


    pA = ps("pA", [128, 8, 128], BF16)
    pB = ps("pB", [128, 512], F32)
    pC = ps("pC", [128, 512], F32)
    pD = ps("pD", [128, D], F32)
    pE = ps("pE", [128, 8, 128], F32)
    pF = ps("pF", [128, 8, DH], F32)

    sd = Sched()
    tap_outs = {}

    def TAP(name, ap, shape, reads):
        if taps is None or name not in taps:
            return
        t = nc.dram_tensor(f"tap_{name}", shape, ap.dtype, kind="ExternalOutput").ap()
        tap_outs[name] = True
        sd.op("SY", lambda e, a=ap, t=t: e.dma_start(t, a), reads=list(reads),
              writes=[f"tapw_{name}"], dma=True)

    C128 = f32(128.0)
    LN127 = f32(np.log(127.0))
    PF128 = 128.0
    EPS = f32(1e-5)
    INV_D = f32(1.0 / D)
    INV127 = f32(1.0 / 127.0)

    sd.op("SY", lambda e: e.dma_start(idb[:], didb), writes=["idb"], dma=True)
    sd.op("SY", lambda e: e.dma_start(msk[:], dmsk.rearrange("p (a f) -> p a f", a=4)),
          writes=["msk"], dma=True)
    sd.op("SY", lambda e: e.dma_start(bqc[:], dbq), writes=["bqc"], dma=True)
    sd.op("SY", lambda e: e.dma_start(bkc[:], dbk), writes=["bkc"], dma=True)
    sd.op("SY", lambda e: e.dma_start(bvr[:], dbv), writes=["bvr"], dma=True)
    sd.op("SY", lambda e: e.dma_start(bor[:], dbo), writes=["bor"], dma=True)
    for nm, dst, src in (("wq", wq, dwq), ("wk", wk, dwk), ("wv", wv, dwv), ("wo", wo, dwo)):
        sd.op("G", lambda e, d=dst, s=src: e.dma_start(
            d[:], s.rearrange("(k p) e -> p k e", p=128)), writes=[nm], dma=True)


    slabbuf = [scr1, x2_[0]]
    slabres = ["scr1", "x2_0"]
    for k in range(8):
        b = k % 2
        sd.op("SY", lambda e, k=k, d=slabbuf[b]: e.dma_start(
            d[:], drhT[128 * k:128 * (k + 1), :]), writes=[slabres[b]], dma=True)
        sd.op("V", lambda e, k=k, b=b: e.reduce_max(
            acc8[:, k:k + 1], slabbuf[b][:], AX.X, apply_absolute_value=True),
            reads=[slabres[b]], writes=[f"acc8_{k}"])
    sd.op("V", lambda e: e.tensor_scalar_max(acc8m[:], acc8[:], 1e-12),
          reads=[f"acc8_{j}" for j in range(8)] + ["acc8"], writes=["acc8m"])
    TAP("acc8", acc8[:], [128, 8], [f"acc8_{j}" for j in range(8)])
    TAP("acc8m", acc8m[:], [128, 8], ["acc8m"])
    for j in range(8):
        sd.op("SY", lambda e, j=j: e.dma_start(
            damax[j:j + 1, :], acc8m[:, j:j + 1],
            allow_slow_non_contiguous=True),
            reads=["acc8m"], writes=[f"damax{j}"], dma=True)
    row3 = damax.rearrange("j q -> () j q")
    def _bcast_load(e):
        d_ap = rcp_rep[:, :].rearrange("p (j q) -> p j q", j=8)
        d_ap, s_ap = broadcast_tensor_aps(d_ap, row3)
        return e.dma_start(d_ap, s_ap)
    sd.op("SY", _bcast_load, reads=[f"damax{j}" for j in range(8)],
          writes=["rcp_rep"], dma=True)
    TAP("amax_rep", rcp_rep[:], [128, D], ["rcp_rep"])
    sd.op("V", lambda e: e.tensor_scalar_mul(rs_rep[:], rcp_rep[:], float(INV127)),
          reads=["rcp_rep"], writes=["rs_rep"])
    sd.op("S", lambda e: e.activation(scr1[:], rcp_rep[:], AF.Ln,
          bias=f32(0.0), scale=f32(1.0)), reads=["rcp_rep", "scr1"], writes=["scr1"])
    sd.op("S", lambda e: e.activation(rcp_rep[:], scr1[:], AF.Exp,
          bias=LN127, scale=f32(-1.0)), reads=["scr1"], writes=["rcp_rep"])
    for k in range(8):
        b = k % 2
        sd.op("SY", lambda e, k=k, d=slabbuf[b]: e.dma_start(
            d[:], drh[128 * k:128 * (k + 1), :]), writes=[slabres[b]], dma=True)
        pdq = ln2
        sd.op("V", lambda e, b=b: e.tensor_mul(yq[:], slabbuf[b][:], rcp_rep[:]),
              reads=[slabres[b], "rcp_rep"], writes=["yq"])
        sd.op("V", lambda e, d=pdq: e.tensor_scalar_add(d[:], yq[:], PF128),
              reads=["yq"], writes=["ln2x"])
        sd.op("V", lambda e, d=pdq: e.tensor_copy(iq[:], d[:]), reads=["ln2x"], writes=["iq"])
        sd.op("G", lambda e: e.tensor_copy(fq[:], iq[:]), reads=["iq"], writes=["fq"])
        sd.op("G", lambda e, d=pdq: e.tensor_sub(yq[:], d[:], fq[:]),
              reads=["ln2x", "fq"], writes=["yq"])
        sd.op("V", lambda e: e.tensor_scalar(iq[:].bitcast(F32), yq[:], 0.5, -128.0, ALU.is_ge, ALU.add),
              reads=["yq"], writes=["iq"])
        sd.op("G", lambda e, k=k: e.tensor_add(qrs[:, k, :], fq[:], iq[:].bitcast(F32)),
              reads=["fq", "iq"], writes=[f"qrs_{k}"])
    TAP("qrs", qrs[:], [128, 8, D], [f"qrs_{k}" for k in range(8)])
    TAP("rcp_rep", rcp_rep[:], [128, D], ["rcp_rep"])
    TAP("rs_rep", rs_rep[:], [128, D], ["rs_rep"])

    def emit_mid_stats(c, p):
        q3 = c % 3
        stt = st_[p]
        stn = f"st{p}"
        x2n, l2n = f"x2_{q3}", "ln2r"
        x2t = x2_[q3]
        sd.op("V", lambda e, s=stt, x2t=x2t: e.scalar_tensor_tensor(
            scr1[:], x2t[:], 1.0, x2t[:], ALU.bypass, ALU.mult,
            accum_out=s[:, 14:15]), reads=[x2n, stn], writes=["scr1", stn])
        sd.op("S", lambda e, s=stt: e.activation(
            s[:, 15:16], s[:, 8:9], AF.Square, bias=f32(0.0), scale=f32(INV_D)),
            reads=[stn], writes=[stn])
        sd.op("V", lambda e, s=stt: e.scalar_tensor_tensor(
            s[:, 10:11], s[:, 14:15], float(INV_D), s[:, 15:16], ALU.mult, ALU.subtract),
            reads=[stn], writes=[stn])
        sd.op("S", lambda e, s=stt: e.activation(
            s[:, 11:12], s[:, 10:11], AF.Ln, bias=EPS, scale=f32(1.0)),
            reads=[stn], writes=[stn])
        sd.op("S", lambda e, s=stt: e.activation(
            s[:, 12:13], s[:, 11:12], AF.Exp, bias=f32(0.0), scale=f32(-0.5)),
            reads=[stn], writes=[stn])
        sd.op("V", lambda e, s=stt: e.scalar_tensor_tensor(
            s[:, 13:14], s[:, 8:9], float(-INV_D), s[:, 12:13], ALU.mult, ALU.mult),
            reads=[stn], writes=[stn])
        sd.op("S", lambda e, s=stt, x2t=x2t: e.activation(
            ln2[:], x2t[:], AF.Identity, bias=s[:, 13:14], scale=s[:, 12:13]),
            reads=[x2n, stn], writes=[l2n])
        TAP(f"ln2_c{c}", ln2[:], [128, D], [l2n])

    def emit_mid_statsB(c, p):
        q3 = c % 3
        stt = st_[p]
        stn = f"st{p}"
        l2n = "ln2r"
        sd.op("V", lambda e, s=stt: e.reduce_max(
            s[:, 16:17], ln2[:], AX.X, apply_absolute_value=True),
            reads=[l2n], writes=[stn])
        sd.op("V", lambda e, s=stt: e.tensor_scalar_max(
            s[:, 17:18], s[:, 16:17], 1e-12), reads=[stn], writes=[stn])
        sd.op("S", lambda e, s=stt: e.activation(
            s[:, 20:21], s[:, 17:18], AF.Ln, bias=f32(0.0), scale=f32(1.0)),
            reads=[stn], writes=[stn])
        sd.op("S", lambda e, s=stt: e.activation(
            s[:, 18:19], s[:, 20:21], AF.Exp, bias=LN127, scale=f32(-1.0)),
            reads=[stn], writes=[stn])
        sd.op("V", lambda e, s=stt, q3=q3: e.tensor_scalar_mul(
            ls3_[q3][:, 0:1], s[:, 17:18], float(INV127)), reads=[stn],
            writes=[f"ls3_{q3}"])
        sd.op("S", lambda e, s=stt: e.activation(
            yq[:], ln2[:], AF.Identity, bias=C128, scale=s[:, 18:19]),
            reads=[l2n, stn], writes=["yq"])
        TAP(f"st2_c{c}", stt[:], [128, 32], [stn])

    def emit_mid_quant(c, p):
        q3 = c % 3
        sd.op("V", lambda e: e.tensor_copy(iq[:], yq[:]), reads=["yq"], writes=["iq"])
        sd.op("G", lambda e: e.tensor_copy(fq[:], iq[:]), reads=["iq"], writes=["fq"])
        sd.op("V", lambda e: e.tensor_sub(scr1[:], yq[:], fq[:]),
              reads=["yq", "fq"], writes=["scr1"])
        sd.op("V", lambda e: e.tensor_scalar(
            iq[:].bitcast(F32), scr1[:], 0.5, -128.0, ALU.is_ge, ALU.add),
            reads=["scr1"], writes=["iq"])
        sd.op("G", lambda e, q3=q3: e.tensor_add(qtm_[q3][:], fq[:], iq[:].bitcast(F32)),
              reads=["fq", "iq"], writes=[f"qtm{q3}"])
        TAP(f"qtm_c{c}", qtm_[q3][:], [128, D], [f"qtm{q3}"])

    def emit_tail(c, cc, p, r0):
        q3 = c % 3
        x2n = f"x2_{q3}"
        for j in range(8):
            sd.op("T", lambda e, j=j, q3=q3: e.transpose(
                pA[:, j, :], qtm_[q3][:, 128 * j:128 * (j + 1)], idb[:]),
                reads=[f"qtm{q3}", "idb"], writes=["pA"])
        sd.op("S", lambda e: e.copy(qTq[:, :, :], pA[:, :, :]),
              reads=["pA"], writes=["qTq"])
        for k in range(8):
            sd.op("T", lambda e, k=k: e.matmul(
                pB[:, 0:512], qTq[:, k, :], qrs[:, k, 0:512],
                start=(k == 0), stop=(k == 7)),
                reads=["qTq", f"qrs_{k}"], writes=["pB"])
            sd.op("T", lambda e, k=k: e.matmul(
                pC[:, 0:512], qTq[:, k, :], qrs[:, k, 512:1024],
                start=(k == 0), stop=(k == 7)),
                reads=["qTq", f"qrs_{k}"], writes=["pC"])
        sd.op("V", lambda e: e.tensor_mul(scr1[:, 0:512], pB[:, :], rs_rep[:, 0:512]),
              reads=["pB", "rs_rep"], writes=["scr1"])
        sd.op("V", lambda e: e.tensor_mul(scr1[:, 512:1024], pC[:, :], rs_rep[:, 512:1024]),
              reads=["pC", "rs_rep"], writes=["scr1"])
        sd.op("V", lambda e, p=p, q3=q3: e.scalar_tensor_tensor(
            ob_[p][:], scr1[:], ls3_[q3][:, 0:1], x2_[q3][:], ALU.mult, ALU.add),
            reads=["scr1", f"ls3_{q3}", x2n], writes=[f"ob{p}"])
        sd.op("SY", lambda e, r0=r0, p=p: e.dma_start(
            dout[r0:r0 + CH, :], ob_[p][:]), reads=[f"ob{p}"], writes=[f"outd{c}"],
            dma=True)

    pending_tail = [None, None]
    pending_mid = [None]

    HB = [64 * (h % 2) for h in range(NH)]
    HS = [h // 2 for h in range(NH)]

    def emit_xdma(c):
        p = c % 2
        r0 = c * CH
        sd.op("SY", lambda e, r0=r0, p=p: e.dma_start(
            xb_[p][:], dx[r0:r0 + CH, :]), writes=[f"xb{p}"], dma=True)

    def emit_ln1_chain(c):
        p = c % 2
        stt = st_[p]
        stn = f"st{p}"
        xr, lr = f"xb{p}", f"ln1_{p}"
        sd.op("V", lambda e, p=p, s=stt: e.reduce_sum(s[:, 0:1], xb_[p][:], AX.X),
              reads=[xr], writes=[stn])
        sd.op("V", lambda e, p=p, s=stt: e.scalar_tensor_tensor(
            scr1[:], xb_[p][:], 1.0, xb_[p][:], ALU.bypass, ALU.mult,
            accum_out=s[:, 6:7]), reads=[xr, stn], writes=["scr1", stn])
        sd.op("G", lambda e, p=p: e.tensor_add(xbo_[p][:], xb_[p][:], bor[:]),
              reads=[xr, "bor"], writes=[f"xbo{p}"])
        sd.op("S", lambda e, s=stt: e.activation(
            s[:, 7:8], s[:, 0:1], AF.Square, bias=f32(0.0), scale=f32(INV_D)),
            reads=[stn], writes=[stn])
        sd.op("V", lambda e, s=stt: e.scalar_tensor_tensor(
            s[:, 2:3], s[:, 6:7], float(INV_D), s[:, 7:8], ALU.mult, ALU.subtract),
            reads=[stn], writes=[stn])
        sd.op("S", lambda e, s=stt: e.activation(
            s[:, 3:4], s[:, 2:3], AF.Ln, bias=EPS, scale=f32(1.0)),
            reads=[stn], writes=[stn])
        sd.op("S", lambda e, s=stt: e.activation(
            s[:, 4:5], s[:, 3:4], AF.Exp, bias=f32(0.0), scale=f32(-0.5)),
            reads=[stn], writes=[stn])
        sd.op("V", lambda e, s=stt: e.scalar_tensor_tensor(
            s[:, 5:6], s[:, 0:1], float(-INV_D), s[:, 4:5], ALU.mult, ALU.mult),
            reads=[stn], writes=[stn])
        sd.op("S", lambda e, p=p, s=stt: e.activation(
            ln1_[p][:], xb_[p][:], AF.Identity, bias=s[:, 5:6], scale=s[:, 4:5]),
            reads=[xr, stn], writes=[lr])
        TAP(f"ln1_c{c}", ln1_[p][:], [128, D], [lr])
        TAP(f"st_c{c}", stt[:], [128, 32], [stn])

    def emit_ln1_T(c):
        p = c % 2
        cc = c % 4
        lr = f"ln1_{p}"
        for j in range(8):
            sd.op("T", lambda e, p=p, j=j: e.transpose(
                pA[:, j, :], ln1_[p][:, 128 * j:128 * (j + 1)], idb[:]),
                reads=[lr, "idb"], writes=["pA"])
        sd.op("S", lambda e, cc=cc: e.copy(
            l1T[:, :, 128 * cc:128 * (cc + 1)], pA[:, :, :]),
            reads=["pA"], writes=[f"l1T_{cc}"])
        TAP(f"l1T_c{c}", l1T[:, :, 128 * cc:128 * (cc + 1)], [128, 8, 128], [f"l1T_{cc}"])

    l1T_all = [f"l1T_{cc}" for cc in range(4)]
    n_macros = n_chunks // 4
    for m in range(n_macros):
        if m == 0:
            for cc in range(4):
                emit_xdma(cc)
                emit_ln1_chain(cc)
                emit_ln1_T(cc)
        for j in range(8):
            for pi, (w, wn, dst, dn, bc, bcn) in enumerate(
                    ((wq, "wq", qTm, "qTm", bqc, "bqc"),
                     (wk, "wk", kTm, "kTm", bkc, "bkc"))):
                bank, bn = (pB, "pB") if (2 * j + pi) % 2 == 0 else (pC, "pC")
                for k in range(8):
                    sd.op("T", lambda e, j=j, k=k, w=w, b=bank: e.matmul(
                        b[:, 0:512], w[:, k, 128 * j:128 * (j + 1)],
                        l1T[:, k, 0:512], start=(k == 0), stop=(k == 7)),
                        reads=[wn] + l1T_all, writes=[bn])
                sd.op("S", lambda e, j=j, d=dst, b=bank, bc=bc: e.activation(
                    d[:, j, :], b[:, 0:512], AF.Identity, bias=bc[:, j:j + 1],
                    scale=f32(1.0)), reads=[bn, bcn], writes=[f"{dn}_{j}"])
        qTm_all = [f"qTm_{j}" for j in range(8)]
        kTm_all = [f"kTm_{j}" for j in range(8)]
        TAP(f"qTm_m{m}", qTm[:], [128, 8, MACRO], qTm_all)
        TAP(f"kTm_m{m}", kTm[:], [128, 8, MACRO], kTm_all)

        for cc in range(4):
            c = 4 * m + cc
            p = c % 2
            cs = c % chunks_per_seq
            r0 = c * CH
            stt = st_[p]
            stn = f"st{p}"
            vtn = f"vtm{cc}"
            if m + 1 < n_macros:
                emit_xdma(c + 4)
            for k in range(8):
                for h2 in range(2):
                    sd.op("T", lambda e, cc=cc, k=k, h2=h2: e.matmul(
                        pE[:, 4 * h2:4 * (h2 + 1), :],
                        l1T[:, k, 128 * cc:128 * (cc + 1)],
                        wv[:, k, 512 * h2:512 * (h2 + 1)],
                        start=(k == 0), stop=(k == 7)),
                        reads=["wv", f"l1T_{cc}"], writes=["pE"])
            sd.op("V", lambda e, cc=cc: e.scalar_tensor_tensor(
                vtm_[cc][:].rearrange("p (a f) -> p a f", a=8),
                pE[:, :, :], 1.0,
                bvr[:].rearrange("p (a f) -> p a f", a=8),
                ALU.bypass, ALU.add),
                reads=["pE", "bvr"], writes=[vtn])
            TAP(f"vtm_c{c}", vtm_[cc][:], [128, D], [vtn])
            for j in range(8):
                sd.op("T", lambda e, cc=cc, j=j: e.transpose(
                    pA[:, j, :], kTm[:, j, 128 * cc:128 * (cc + 1)], idb[:]),
                    reads=[f"kTm_{j}", "idb"], writes=["pA"])
            sd.op("S", lambda e: e.copy(ktm[:, :, :], pA[:, :, :]),
                  reads=["pA"], writes=["ktm"])
            TAP(f"ktm_c{c}", ktm[:], [128, 8, 128], ["ktm"])
            if cs > 0:
                sd.op("S", lambda e: e.copy(Ssb[:, :, :], pF[:, :, :]),
                      reads=["pF"], writes=["Ssb"])
            for lh in range(8):
                h = lh
                bank, bn = (pB, "pB") if h % 2 == 0 else (pC, "pC")
                a = lh // 2
                sd.op("T", lambda e, cc=cc, h=h, b=bank, a=a: e.matmul(
                    b[:, 128 * a:128 * (a + 1)],
                    kTm[HB[h]:HB[h] + DH, HS[h], 128 * cc:128 * (cc + 1)],
                    qTm[HB[h]:HB[h] + DH, HS[h], 128 * cc:128 * (cc + 1)],
                    start=True, stop=True,
                    tile_position=(HB[h], 0), skip_group_check=True),
                    reads=[f"kTm_{HS[h]}", f"qTm_{HS[h]}"], writes=[bn])
            for lh in range(8):
                h = 8 + lh
                a = (lh // 2) + (0 if lh % 2 == 0 else 4)
                sd.op("T", lambda e, cc=cc, h=h, a=a: e.matmul(
                    pD[:, 128 * a:128 * (a + 1)],
                    kTm[HB[h]:HB[h] + DH, HS[h], 128 * cc:128 * (cc + 1)],
                    qTm[HB[h]:HB[h] + DH, HS[h], 128 * cc:128 * (cc + 1)],
                    start=True, stop=True,
                    tile_position=(HB[h], 0), skip_group_check=True),
                    reads=[f"kTm_{HS[h]}", f"qTm_{HS[h]}"], writes=["pD"])
            sd.op("V", lambda e: e.tensor_tensor(
                sTm_[0][:, 0:4, :], pB[:, :].rearrange("p (a f) -> p a f", a=4),
                msk[:, :, :], ALU.mult),
                reads=["pB", "msk"], writes=["sTm0"])
            sd.op("V", lambda e: e.tensor_tensor(
                sTm_[0][:, 4:8, :], pC[:, :].rearrange("p (a f) -> p a f", a=4),
                msk[:, :, :], ALU.mult),
                reads=["pC", "msk"], writes=["sTm0"])
            sd.op("V", lambda e: e.tensor_tensor(
                sTm_[1][:, 0:4, :], pD[:, 0:512].rearrange("p (a f) -> p a f", a=4),
                msk[:, :, :], ALU.mult),
                reads=["pD", "msk"], writes=["sTm1"])
            sd.op("V", lambda e: e.tensor_tensor(
                sTm_[1][:, 4:8, :], pD[:, 512:1024].rearrange("p (a f) -> p a f", a=4),
                msk[:, :, :], ALU.mult),
                reads=["pD", "msk"], writes=["sTm1"])
            if pending_mid[0] is not None:
                emit_mid_stats(*pending_mid[0])
                emit_mid_statsB(*pending_mid[0])
            for h in range(NH):
                w = h // 8
                lh = h - 8 * w
                slab = (lh // 2) + (0 if lh % 2 == 0 else 4)
                first = h in (0, 1, 8, 9)
                sd.op("T", lambda e, cc=cc, h=h, first=first, w=w, sl=slab: e.matmul(
                    pE[HB[h]:HB[h] + DH, HS[h], :],
                    vtm_[cc][:, DH * h:DH * (h + 1)],
                    sTm_[w][:, sl, :],
                    start=first, stop=(cs == 0),
                    tile_position=(0, HB[h]), skip_group_check=True),
                    reads=[vtn, f"sTm{w}"], writes=["pE"])
                if cs > 0:
                    sd.op("T", lambda e, cc=cc, h=h: e.matmul(
                        pE[HB[h]:HB[h] + DH, HS[h], :],
                        Ssb[HB[h]:HB[h] + DH, HS[h], :],
                        qTm[HB[h]:HB[h] + DH, HS[h], 128 * cc:128 * (cc + 1)],
                        start=False, stop=True,
                        tile_position=(HB[h], HB[h]), skip_group_check=True),
                        reads=["Ssb", f"qTm_{HS[h]}"], writes=["pE"])
            for h in range(NH):
                sd.op("T", lambda e, cc=cc, h=h, cs=cs: e.matmul(
                    pF[HB[h]:HB[h] + DH, HS[h], :],
                    ktm[:, HS[h], HB[h]:HB[h] + DH],
                    vtm_[cc][:, DH * h:DH * (h + 1)],
                    start=(cs == 0 and h in (0, 1)),
                    stop=(cs == chunks_per_seq - 1 and h in (NH - 2, NH - 1)),
                    tile_position=(0, HB[h]), skip_group_check=True),
                    reads=["ktm", vtn], writes=["pF"])
            sd.op("S", lambda e: e.copy(aT[:, :, :], pE[:, :, :]),
                  reads=["pE"], writes=["aT"])
            TAP(f"sTm0_c{c}", sTm_[0][:], [128, 8, 128], ["sTm0"])
            TAP(f"sTm1_c{c}", sTm_[1][:], [128, 8, 128], ["sTm1"])
            TAP(f"aT_c{c}", aT[:], [128, 8, 128], ["aT"])
            if cs > 0:
                TAP(f"Ssb_c{c}", Ssb[:], [128, 8, DH], ["Ssb"])
            if m + 1 < n_macros:
                emit_ln1_chain(c + 4)
            if pending_mid[0] is not None:
                emit_mid_quant(*pending_mid[0])
                pending_mid[0] = None
            for k in range(8):
                for h2 in range(2):
                    sd.op("T", lambda e, k=k, h2=h2: e.matmul(
                        pD[:, 512 * h2:512 * (h2 + 1)], aT[:, k, :],
                        wo[:, k, 512 * h2:512 * (h2 + 1)],
                        start=(k == 0), stop=(k == 7)),
                        reads=["aT", "wo"], writes=["pD"])
            q3 = c % 3
            sd.op("V", lambda e, p=p, q3=q3, s=stt: e.scalar_tensor_tensor(
                x2_[q3][:], pD[:, :], 1.0, xbo_[p][:], ALU.bypass, ALU.add,
                accum_out=s[:, 8:9]), reads=["pD", f"xbo{p}", stn],
                writes=[f"x2_{q3}", stn])
            TAP(f"x2_c{c}", x2_[q3][:], [128, D], [f"x2_{q3}"])
            if pending_tail[0] is not None:
                emit_tail(*pending_tail[0])
            if m + 1 < n_macros:
                emit_ln1_T(c + 4)
            pending_mid[0] = (c, p)
            pending_tail[0] = pending_tail[1]
            pending_tail[1] = (c, cc, p, r0)

    if pending_mid[0] is not None:
        emit_mid_stats(*pending_mid[0])
        emit_mid_statsB(*pending_mid[0])
        emit_mid_quant(*pending_mid[0])
        pending_mid[0] = None
    if pending_tail[0] is not None:
        emit_tail(*pending_tail[0])
    if pending_tail[1] is not None:
        emit_tail(*pending_tail[1])

    sd.finalize()
    with (
        nc.semaphore("s_t") as s_t,
        nc.semaphore("s_v") as s_v,
        nc.semaphore("s_s") as s_s,
        nc.semaphore("s_g") as s_g,
        nc.semaphore("s_ds") as s_ds,
        nc.semaphore("s_dg") as s_dg,
        nc.Block() as block,
    ):
        sd.sems = {"T": s_t, "V": s_v, "S": s_s, "G": s_g, "DS": s_ds, "DG": s_dg}
        total_ds = 16 * sd.total["DS"]

        def run_stream(eng_obj, stream):
            hw = {}
            for o in sd.ops:
                if o["stream"] != stream:
                    continue
                for dtag, didx in o["waits"]:
                    rk = sd.rank[dtag][didx]
                    if hw.get(dtag, 0) >= rk:
                        continue
                    hw[dtag] = rk
                    val = 16 * rk if dtag in ("DS", "DG") else rk
                    eng_obj.wait_ge(sd.sems[dtag], val)
                tag = o["tag"]
                if o["idx"] in sd.needed[tag]:
                    rk = sd.rank[tag][o["idx"]]
                    if o["dma"]:
                        o["fn"](eng_obj).then_inc(sd.sems[tag], 16)
                    elif stream == "T":
                        o["fn"](eng_obj).then_inc(sd.sems[tag], 1)
                    else:
                        o["fn"](eng_obj).then_inc(sd.sems[tag], 1)
                        eng_obj.wait_ge(sd.sems[tag], rk)
                else:
                    o["fn"](eng_obj)

        @block.tensor
        def _(t):
            run_stream(t, "T")

        @block.vector
        def _(v):
            run_stream(v, "V")

        @block.scalar
        def _(s):
            run_stream(s, "S")

        @block.gpsimd
        def _(g):
            run_stream(g, "G")

        @block.sync
        def _(sy):
            run_stream(sy, "SY")
            sy.wait_ge(s_ds, total_ds)

    es.close()
    return nc


def _host_inputs(inputs, x_shard):
    wq = np.ascontiguousarray(inputs["L0_attn_wq_w"].T)
    wk = np.ascontiguousarray(inputs["L0_attn_wk_w"].T)
    wv = np.ascontiguousarray(inputs["L0_attn_wv_w"].T)
    wo = np.ascontiguousarray(inputs["L0_attn_wo_w"].T)
    rhs = np.ascontiguousarray(inputs["L0_ffn_rhs"])
    bq = np.ascontiguousarray(inputs["L0_attn_wq_b"].reshape(8, 128).T)
    bk = np.ascontiguousarray(inputs["L0_attn_wk_b"].reshape(8, 128).T)
    bvrep = np.ascontiguousarray(
        np.broadcast_to(inputs["L0_attn_wv_b"][None, :], (128, D)))
    borep = np.ascontiguousarray(
        np.broadcast_to(inputs["L0_attn_wo_b"][None, :], (128, D)))
    mask = np.triu(np.ones((128, 128), dtype=np.float32))
    mask4 = np.ascontiguousarray(np.tile(mask, (1, 4)))
    identb = np.eye(128).astype(ml_dtypes.bfloat16)
    return {
        "x": x_shard, "wqT": wq, "wkT": wk, "wvT": wv, "woT": wo, "rhs": rhs,
        "bqc": bq, "bkc": bk, "bvrep": bvrep, "borep": borep,
        "mask4": mask4, "identb": identb, "identf": np.eye(128, dtype=np.float32),
        "rhsT": np.ascontiguousarray(rhs.T),
    }


def kernel(**inputs):
    x = np.asarray(inputs["L0_attn_x"], dtype=np.float32)
    B, L, d = x.shape
    assert (B, L, d) == (32, 2048, 1024)
    xf = np.ascontiguousarray(x.reshape(B * L, d))
    seq_per_core = B // N_CORES
    rows_per_core = seq_per_core * L
    nc = build_program(n_seq=seq_per_core, chunks_per_seq=L // CH)
    in_maps = []
    for ci in range(N_CORES):
        shard = np.ascontiguousarray(
            xf[ci * rows_per_core:(ci + 1) * rows_per_core])
        in_maps.append(_host_inputs(inputs, shard))
    res = run_bass_kernel_spmd(nc, in_maps, core_ids=list(range(N_CORES)))
    out = np.concatenate([np.asarray(r["out"]) for r in res.results], axis=0)
    return out.reshape(B, L, d).astype(np.float32)


if __name__ == "__main__":
    rng = np.random.RandomState(0)
    ins = {
        "L0_attn_x": rng.randn(32, 2048, 1024).astype(np.float32),
        "L0_attn_wq_w": (rng.randn(1024, 1024) / 32).astype(np.float32),
        "L0_attn_wq_b": (rng.randn(1024) * 0.01).astype(np.float32),
        "L0_attn_wk_w": (rng.randn(1024, 1024) / 32).astype(np.float32),
        "L0_attn_wk_b": (rng.randn(1024) * 0.01).astype(np.float32),
        "L0_attn_wv_w": (rng.randn(1024, 1024) / 32).astype(np.float32),
        "L0_attn_wv_b": (rng.randn(1024) * 0.01).astype(np.float32),
        "L0_attn_wo_w": (rng.randn(1024, 1024) / 32).astype(np.float32),
        "L0_attn_wo_b": (rng.randn(1024) * 0.01).astype(np.float32),
        "L0_ffn_rhs": rng.randn(1024, 1024).astype(np.float32),
    }
    out = kernel(**ins)
    print(out.shape, out.dtype)
